# version 04
# baseline (speedup 1.0000x reference)
import numpy as np
import ml_dtypes
from contextlib import ExitStack

import concourse.bass as bass
import concourse.mybir as mybir
from concourse.bass_utils import run_bass_kernel_spmd

F32 = mybir.dt.float32
BF16 = mybir.dt.bfloat16
I32 = mybir.dt.int32
AF = mybir.ActivationFunctionType
ALU = mybir.AluOpType
AX = mybir.AxisListType

T = 2048
D = 1024
HFF = 3072
NT = T // 128
ND = D // 128
HEADS = 16
DH = 64
CH = 128
BLK = 512
NBLK = T // BLK
NCH = BLK // CH
HC = 512
NHC = HFF // HC
LN_EPS = 1e-5
ATT_EPS = 1e-6
DOWN_SCALE = 0.5 / (127.0 * 127.0)
SILU_SCALE = 1.0 / 127.0

PHASES = ("ffn1", "attn", "ffn2")
DBG_LN = False
PARANOIA = 1
DBG_ATTN = False

R0 = 0
XCS = 16384
SUMS = 18432
NEGM = 18448
VARS = 18464
RSTD = 18480
DSB = 18496
DTOT = 18512
DCL = 18528
REC = 18544
ZF = 18560
BQC = 18576
BKC = 18584
EPSC = 18592
NF32 = 18600

SIL0, SIL1 = 0, 512
LNT = 1024
WG0, WG1 = 17408, 21504
WU0, WU1 = 25600, 29696
WD0, WD1 = 33792, 37888
H0, H1 = 41984, 44032
FFN_END = 46080
LN2T = 0
QF = 4096
KF = 8192
KTM = 12288
VSB = 13312
OTM = 17408
OFM = 21504
ATS = 25600
SSB = 27648
ZSB = 28672
WQT = 28688
WKT = 36880
WVT = 45072
WOT = 53264
NBF = 61456


def build_nc():
    nc = bass.Bass("TRN2", target_bir_lowering=False)

    x_in = nc.dram_tensor("x", [T, D], F32, kind="ExternalInput").ap()
    wq = nc.dram_tensor("wq", [D, D], F32, kind="ExternalInput").ap()
    wk = nc.dram_tensor("wk", [D, D], F32, kind="ExternalInput").ap()
    wv = nc.dram_tensor("wv", [D, D], F32, kind="ExternalInput").ap()
    wo = nc.dram_tensor("wo", [D, D], F32, kind="ExternalInput").ap()
    bq = nc.dram_tensor("bq", [D], F32, kind="ExternalInput").ap()
    bk = nc.dram_tensor("bk", [D], F32, kind="ExternalInput").ap()
    bv = nc.dram_tensor("bv", [D], F32, kind="ExternalInput").ap()
    bo = nc.dram_tensor("bo", [D], F32, kind="ExternalInput").ap()
    wg = nc.dram_tensor("wg", [HFF, D], I32, kind="ExternalInput").ap()
    wu = nc.dram_tensor("wu", [HFF, D], I32, kind="ExternalInput").ap()
    wd = nc.dram_tensor("wd", [D, HFF], I32, kind="ExternalInput").ap()
    out = nc.dram_tensor("out", [T, D], F32, kind="ExternalOutput").ap()

    wgc = nc.dram_tensor("wgc", [HFF, D], BF16).ap()
    wuc = nc.dram_tensor("wuc", [HFF, D], BF16).ap()
    wdc = nc.dram_tensor("wdc", [D, HFF], BF16).ap()
    wqc = nc.dram_tensor("wqc", [D, D], BF16).ap()
    wkc = nc.dram_tensor("wkc", [D, D], BF16).ap()
    wvc = nc.dram_tensor("wvc", [D, D], BF16).ap()
    woc = nc.dram_tensor("woc", [D, D], BF16).ap()
    lnd = nc.dram_tensor("lnd", [T, D], BF16).ap()

    with ExitStack() as stk:
        E = stk.enter_context
        AF32 = E(nc.sbuf_tensor("AF32", [128, NF32], F32))
        ABF = E(nc.sbuf_tensor("ABF", [128, NBF], BF16))
        mask = E(nc.sbuf_tensor("mask", [128, 128], BF16))
        iden = E(nc.sbuf_tensor("iden", [128, 128], BF16))
        iden8 = E(nc.sbuf_tensor("iden8", [8, 8], F32))
        iotas = E(nc.sbuf_tensor("iotas", [128, 128], F32))
        ones_l = E(nc.sbuf_tensor("ones_l", [1, 128], BF16))
        ones_c = E(nc.sbuf_tensor("ones_c", [128, 1], BF16))
        b8a = E(nc.sbuf_tensor("b8a", [8, 128], F32))
        b8b = E(nc.sbuf_tensor("b8b", [8, 128], F32))
        lstg = E(nc.sbuf_tensor("lstg", [128, 2048], BF16))
        zer = E(nc.sbuf_tensor("zer", [128, 128], BF16))
        PB = [E(nc.psum_tensor(f"PB{i}", [128, 512], F32)) for i in range(7)]
        PT = E(nc.psum_tensor("PT", [128, 1024], BF16))
        sem = {s: E(nc.semaphore(s)) for s in ("pe", "dve", "act", "pool", "dls", "dca")}
        block = E(nc.Block())

        ops = {k: [] for k in ("sync", "gpsimd", "vector", "scalar", "tensor")}
        cnt = {k: 0 for k in sem}

        def OP(eng, builder, inc=None, n=1):
            if inc is not None:
                cnt[inc] += n
            def run(e, b=builder, s=inc, m=n):
                inst = b(e)
                if s is not None:
                    inst.then_inc(sem[s], m)
            ops[eng].append(run)
            return cnt[inc] if inc is not None else None

        def W(eng, s, v):
            assert v <= cnt[s], f"wait on future value: {s} {v} > {cnt[s]}"
            if v <= 0:
                return
            ops[eng].append(lambda e, s=s, v=v: e.wait_ge(sem[s], v))

        def BAR():
            for eng in ("tensor", "vector", "scalar", "sync", "gpsimd"):
                for s in ("pe", "dve", "act", "dls", "dca", "pool"):
                    if cnt[s] > 0:
                        W(eng, s, cnt[s])

        def f32(c0, w):
            return AF32[:, c0:c0 + w]

        def bf(c0, w):
            return ABF[:, c0:c0 + w]

        OP("gpsimd", lambda e: e.iota(iotas[:], [[1, 128]], channel_multiplier=-1,
                                      allow_small_or_imprecise_dtypes=True), inc="pool")
        OP("gpsimd", lambda e: e.memset(ones_l[:], 1.0))
        OP("gpsimd", lambda e: e.memset(ones_c[:], 1.0))
        OP("gpsimd", lambda e: e.memset(zer[:], 0.0))
        OP("gpsimd", lambda e: e.memset(f32(ZF, 16), 0.0))
        OP("gpsimd", lambda e: e.memset(bf(SSB, 1024), 0.0))
        OP("gpsimd", lambda e: e.memset(bf(ZSB, 16), 0.0))
        OP("gpsimd", lambda e: e.memset(f32(EPSC, 1), LN_EPS), inc="pool")
        W("vector", "pool", 1)
        OP("vector", lambda e: e.tensor_single_scalar(mask[:], iotas[:], 0.0, ALU.is_ge))
        OP("vector", lambda e: e.tensor_single_scalar(iden[:], iotas[:], 0.0, ALU.is_equal))
        OP("vector", lambda e: e.tensor_single_scalar(iden8[:], iotas[0:8, 0:8], 0.0, ALU.is_equal),
           inc="dve")
        consts_dve = cnt["dve"]
        for eng in ("tensor", "vector", "scalar"):
            W(eng, "pool", 2)
        W("tensor", "dve", consts_dve)
        W("scalar", "dve", consts_dve)

        cast_ms = {}
        OP("gpsimd", lambda e: e.dma_start(wdc, wd), inc="dca", n=16)
        for c in range(NHC):
            OP("gpsimd", lambda e, c=c: e.dma_start(wgc[c * HC:(c + 1) * HC, :],
                                                    wg[c * HC:(c + 1) * HC, :]), inc="dca", n=16)
            OP("gpsimd", lambda e, c=c: e.dma_start(wuc[c * HC:(c + 1) * HC, :],
                                                    wu[c * HC:(c + 1) * HC, :]), inc="dca", n=16)
            cast_ms[c] = cnt["dca"]
        for cdst, csrc in ((wqc, wq), (wkc, wk), (wvc, wv), (woc, wo)):
            OP("gpsimd", lambda e, a=cdst, b=csrc: e.dma_start(a, b), inc="dca", n=16)
        cast_ms["attn"] = cnt["dca"]

        x_ld = OP("sync", lambda e: e.dma_start(
            f32(R0, 16384).rearrange("p (t d) -> p t d", t=NT),
            x_in.rearrange("(t p) d -> p t d", p=128)), inc="dls", n=16)


        OP("sync", lambda e: e.dma_start(b8a[:], bq.rearrange("(a b) -> a b", a=8)),
           inc="dls", n=16)
        OP("sync", lambda e: e.dma_start(b8b[:], bk.rearrange("(a b) -> a b", a=8)),
           inc="dls", n=16)
        bqk_ld = cnt["dls"]
        W("tensor", "dls", bqk_ld)
        OP("tensor", lambda e: e.transpose(PB[0][0:128, 0:8], b8a[0:8, :], iden8[:]))
        tr1 = OP("tensor", lambda e: e.matmul(
            PB[1][:, 0:128], iden[:, :], mask[:, 0:128],
            start=True, stop=True), inc="pe")
        W("scalar", "pe", tr1)
        cp1 = OP("scalar", lambda e: e.copy(f32(BQC, 8), PB[0][0:128, 0:8]), inc="act")
        W("tensor", "act", cp1)
        OP("tensor", lambda e: e.transpose(PB[0][0:128, 0:8], b8b[0:8, :], iden8[:]))
        tr2 = OP("tensor", lambda e: e.matmul(
            PB[1][:, 0:128], iden[:, :], mask[:, 0:128],
            start=True, stop=True), inc="pe")
        W("scalar", "pe", tr2)
        bias_done = OP("scalar", lambda e: e.copy(f32(BKC, 8), PB[0][0:128, 0:8]), inc="act")

        def emit_ln(wait_dve=0, wait_pe=0, wait_act=0):
            if PARANOIA:
                BAR()
            ln_in_ld = cnt["dls"] if cnt["dls"] > x_ld else x_ld
            """LN(r) for all 16 tiles -> lnd (DRAM, token-major bf16).
            ln = r*rstd + (negm*rstd) via a single two-op tensor_scalar."""
            W("vector", "dls", ln_in_ld)
            if wait_pe:
                W("vector", "pe", wait_pe)
            if wait_dve:
                W("vector", "dve", wait_dve)
            for i in range(NT):
                OP("vector", lambda e, i=i: e.reduce_sum(
                    f32(SUMS + i, 1), f32(R0 + i * D, D), axis=AX.X), inc="dve")
            sums_done = cnt["dve"]
            if wait_act:
                W("scalar", "act", wait_act)
            W("scalar", "dve", sums_done)
            OP("scalar", lambda e: e.mul(f32(NEGM, 16), f32(SUMS, 16), -1.0 / D), inc="act")
            W("scalar", "act", cnt["act"])
            sq_ms = {}
            for i in range(NT):
                par = i % 2
                if i - 2 in sq_ms:
                    W("scalar", "dve", sq_ms[i - 2])
                OP("scalar", lambda e, i=i, par=par: e.activation(
                    f32(XCS + par * 1024, D), f32(R0 + i * D, D), AF.Identity,
                    bias=f32(NEGM + i, 1), scale=1.0), inc="act")
                a = cnt["act"]
                W("vector", "act", a)
                OP("vector", lambda e, par=par, i=i: e.scalar_tensor_tensor(
                    lstg[:, 0:1024], f32(XCS + par * 1024, D), 0.0, f32(XCS + par * 1024, D),
                    ALU.bypass, ALU.mult, accum_out=f32(VARS + i, 1)), inc="dve")
                sq_ms[i] = cnt["dve"]
            W("scalar", "dve", cnt["dve"])
            OP("scalar", lambda e: e.activation(f32(DCL, 16), f32(VARS, 16), AF.Ln,
                                                bias=f32(EPSC, 1), scale=1.0 / D), inc="act")
            W("scalar", "act", cnt["act"])
            OP("scalar", lambda e: e.activation(f32(RSTD, 16), f32(DCL, 16), AF.Exp,
                                                scale=-0.5), inc="act")
            rstd_done = cnt["act"]
            W("vector", "act", rstd_done)
            OP("vector", lambda e: e.tensor_tensor(f32(NEGM, 16), f32(NEGM, 16),
                                                   f32(RSTD, 16), ALU.mult), inc="dve")
            W("vector", "dve", cnt["dve"])
            st_ms = {}
            for i in range(NT):
                par = i % 2
                if i - 2 in st_ms:
                    W("vector", "dls", st_ms[i - 2])
                OP("vector", lambda e, i=i, par=par: e.tensor_scalar(
                    lstg[:, par * 1024:(par + 1) * 1024], f32(R0 + i * D, D),
                    f32(RSTD + i, 1), f32(NEGM + i, 1), ALU.mult, ALU.add), inc="dve")
                a = cnt["dve"]
                W("sync", "dve", a)
                OP("sync", lambda e, i=i, par=par: e.dma_start(
                    lnd[i * 128:(i + 1) * 128, :], lstg[:, par * 1024:(par + 1) * 1024]),
                    inc="dls", n=16)
                st_ms[i] = cnt["dls"]
            return cnt["dls"]

        def emit_ffn(ln_stores_done, first, pre_pe=0, pre_dve=0, pre_act=0):
            if PARANOIA:
                BAR()
            if pre_pe:
                W("sync", "pe", pre_pe)
            if pre_dve:
                W("sync", "dve", pre_dve)
            W("sync", "dls", ln_stores_done)
            for ki in range(ND):
                OP("sync", lambda e, ki=ki: e.dma_start(
                    bf(LNT + ki * 2048, 2048), lnd[:, ki * 128:(ki + 1) * 128],
                    transpose=True), inc="dls", n=16)
            lnt_done = cnt["dls"]

            wbg, wbu, wbd = (WG0, WG1), (WU0, WU1), (WD0, WD1)
            ld_ms, pe_ms = {}, {}

            def emit_wload(c):
                par = c % 2
                if first:
                    W("sync", "dca", cast_ms[c])
                if c - 2 in pe_ms:
                    W("sync", "pe", pe_ms[c - 2])
                for ki in range(ND):
                    OP("sync", lambda e, c=c, ki=ki, par=par: e.dma_start(
                        bf(wbg[par] + ki * 512, 512),
                        wgc[c * HC:(c + 1) * HC, ki * 128:(ki + 1) * 128],
                        transpose=True), inc="dls", n=16)
                for ki in range(ND):
                    OP("sync", lambda e, c=c, ki=ki, par=par: e.dma_start(
                        bf(wbu[par] + ki * 512, 512),
                        wuc[c * HC:(c + 1) * HC, ki * 128:(ki + 1) * 128],
                        transpose=True), inc="dls", n=16)
                for ht in range(4):
                    OP("sync", lambda e, c=c, ht=ht, par=par: e.dma_start(
                        bf(wbd[par] + ht * 1024, 1024),
                        wdc[:, c * HC + ht * 128:c * HC + (ht + 1) * 128],
                        transpose=True), inc="dls", n=16)
                ld_ms[c] = cnt["dls"]

            emit_wload(0)
            silu_ms, h_ms, stt_ms = {}, {}, {}
            first_mm = True
            for c in range(NHC):
                par = c % 2
                if c + 1 < NHC:
                    emit_wload(c + 1)
                W("tensor", "dls", max(ld_ms[c], lnt_done))
                if first_mm:
                    if pre_act:
                        W("tensor", "act", pre_act)
                    if pre_dve:
                        W("tensor", "dve", pre_dve)
                    first_mm = False
                for cc in range(4):
                    for ht in range(4):
                        gbank = PB[0] if ht % 2 == 0 else PB[4]
                        ubank = PB[1] if ht % 2 == 0 else PB[5]
                        key = (c, cc, ht)
                        prev = (c, cc, ht - 2) if ht >= 2 else (
                            (c, cc - 1, ht + 2) if cc else (c - 1, 3, ht + 2))
                        if prev in silu_ms:
                            W("tensor", "act", silu_ms[prev])
                        if prev in h_ms:
                            W("tensor", "dve", h_ms[prev])
                        for ki in range(ND):
                            OP("tensor", lambda e, par=par, ki=ki, ht=ht, cc=cc, g=gbank: e.matmul(
                                g[:, :], bf(wbg[par] + ki * 512 + ht * 128, 128),
                                bf(LNT + ki * 2048 + cc * 512, 512),
                                start=(ki == 0), stop=(ki == ND - 1)))
                        for ki in range(ND):
                            OP("tensor", lambda e, par=par, ki=ki, ht=ht, cc=cc, u=ubank: e.matmul(
                                u[:, :], bf(wbu[par] + ki * 512 + ht * 128, 128),
                                bf(LNT + ki * 2048 + cc * 512, 512),
                                start=(ki == 0), stop=(ki == ND - 1)))
                        OP("tensor", lambda e: e.matmul(
                            PB[6][:, 0:512], iden[:, :], bf(LNT, 512),
                            start=True, stop=True), inc="pe")
                        u_pe = cnt["pe"]
                        W("scalar", "pe", u_pe)
                        if prev in h_ms:
                            W("scalar", "dve", h_ms[prev])
                        OP("scalar", lambda e, ht=ht, g=gbank: e.activation(
                            bf(SIL0 + (ht % 2) * 512, 512), g[:, :], AF.Silu,
                            scale=SILU_SCALE), inc="act")
                        silu_ms[key] = cnt["act"]
                        W("vector", "act", silu_ms[key])
                        W("vector", "pe", u_pe)
                        OP("vector", lambda e, ht=ht, cc=cc, u=ubank: e.tensor_tensor(
                            bf((H0 if cc % 2 == 0 else H1) + ht * 512, 512),
                            bf(SIL0 + (ht % 2) * 512, 512), u[:, :], ALU.mult), inc="dve")
                        h_ms[key] = cnt["dve"]
                    hdone = cnt["dve"]
                    W("tensor", "dve", hdone)
                    for j in range(4):
                        t = cc * 4 + j
                        for half in range(2):
                            bkey = (j * 2 + half) % 2
                            dbank = PB[2] if bkey == 0 else PB[3]
                            if bkey in stt_ms:
                                W("tensor", "dve", stt_ms[bkey])
                            for ht in range(4):
                                OP("tensor", lambda e, ht=ht, cc=cc, j=j, half=half, par=par, d=dbank: e.matmul(
                                    d[:, :],
                                    bf((H0 if cc % 2 == 0 else H1) + ht * 512 + j * 128, 128),
                                    bf(wbd[par] + ht * 1024 + half * 512, 512),
                                    start=(ht == 0), stop=(ht == 3)))
                            OP("tensor", lambda e: e.matmul(
                                PB[6][:, 0:512], iden[:, :], bf(LNT, 512),
                                start=True, stop=True), inc="pe")
                            d_pe = cnt["pe"]
                            W("vector", "pe", d_pe)
                            OP("vector", lambda e, t=t, half=half, d=dbank: e.scalar_tensor_tensor(
                                f32(R0 + t * D + half * 512, 512), d[:, :], DOWN_SCALE,
                                f32(R0 + t * D + half * 512, 512), ALU.mult, ALU.add),
                               inc="dve")
                            stt_ms[bkey] = cnt["dve"]
                pe_ms[c] = cnt["pe"]
            return cnt["dve"], cnt["pe"], cnt["act"]

        def emit_attn(ln_stores_done, pre_pe, pre_dve, pre_act):
            if PARANOIA:
                BAR()
            W("gpsimd", "dls", ln_stores_done)
            W("gpsimd", "pe", pre_pe)
            W("gpsimd", "dve", pre_dve)
            OP("gpsimd", lambda e: e.memset(f32(ZF, 16), 0.0))
            OP("gpsimd", lambda e: e.memset(bf(SSB, 1024), 0.0))
            OP("gpsimd", lambda e: e.memset(bf(ZSB, 16), 0.0), inc="pool")
            sz_ms = cnt["pool"]
            if pre_dve:
                W("tensor", "dve", pre_dve)
            if pre_act:
                W("tensor", "act", pre_act)
            OP("tensor", lambda e: e.matmul(PB[2][:, :], zer[:, :], bf(LNT, 512),
                                            start=True, stop=True))
            OP("tensor", lambda e: e.matmul(PB[3][:, :], zer[:, :], bf(LNT, 512),
                                            start=True, stop=True))
            OP("gpsimd", lambda e: e.dma_start(
                lstg[:, 0:1024], bass.AP(bv.tensor, 0, [[0, 128], [1, D]])), inc="dca", n=16)
            OP("gpsimd", lambda e: e.dma_start(
                lstg[:, 1024:2048], bass.AP(bo.tensor, 0, [[0, 128], [1, D]])), inc="dca", n=16)
            bias_full = cnt["dca"]
            W("sync", "dca", cast_ms["attn"])
            for ki in range(ND):
                OP("sync", lambda e, ki=ki: e.dma_start(
                    bf(WOT + ki * 1024, 1024), woc[:, ki * 128:(ki + 1) * 128],
                    transpose=True), inc="dls", n=16)
            W("sync", "pe", pre_pe)
            W("sync", "dve", pre_dve)
            for wdst, wsrc in ((WQT, wqc), (WKT, wkc), (WVT, wvc)):
                for ki in range(ND):
                    OP("sync", lambda e, ki=ki, wdst=wdst, wsrc=wsrc: e.dma_start(
                        bf(wdst + ki * 1024, 1024), wsrc[:, ki * 128:(ki + 1) * 128],
                        transpose=True), inc="dls", n=16)
            attw_done = cnt["dls"]

            PROJ_BANKS = (0, 1, 4, 5)
            projend = {}
            prev_block_dve = pre_dve
            core_dve_end = 0
            for b in range(NBLK):
                if b == 0:
                    W("sync", "dls", ln_stores_done)
                else:
                    W("sync", "pe", projend[b - 1])
                for ft in range(ND):
                    OP("sync", lambda e, b=b, ft=ft: e.dma_start(
                        bf(LN2T + ft * 512, 512),
                        lnd[b * BLK:(b + 1) * BLK, ft * 128:(ft + 1) * 128],
                        transpose=True), inc="dls", n=16)
                ln2_done = cnt["dls"]

                W("tensor", "dls", max(ln2_done, attw_done))
                W("tensor", "dve", prev_block_dve)
                if b == 0:
                    W("tensor", "act", max(bias_done, pre_act))
                    W("tensor", "pool", sz_ms)
                    W("vector", "pool", sz_ms)
                    W("vector", "dca", bias_full)
                evac_act = {}
                evac_dve = {}
                elu_dve = {}
                bi = 0
                for wt, bc, dst in ((WQT, BQC, QF), (WKT, BKC, KF)):
                    for ft in range(ND):
                        bidx = PROJ_BANKS[bi % 4]
                        par = bi % 2
                        bi += 1
                        bank = PB[bidx]
                        if bidx in evac_dve:
                            W("tensor", "dve", evac_dve[bidx])
                        for ki in range(ND):
                            OP("tensor", lambda e, wt=wt, ki=ki, ft=ft, bank=bank: e.matmul(
                                bank[:, :], bf(wt + ki * 1024 + ft * 128, 128),
                                bf(LN2T + ki * 512, 512),
                                start=(ki == 0), stop=(ki == ND - 1)))
                        OP("tensor", lambda e: e.matmul(
                            PB[6][:, 0:512], iden[:, :], bf(LN2T, 512),
                            start=True, stop=True), inc="pe")
                        p_pe = cnt["pe"]
                        W("scalar", "pe", p_pe)
                        if par in elu_dve:
                            W("scalar", "dve", elu_dve[par])
                        OP("scalar", lambda e, bank=bank, bc=bc, ft=ft, par=par: e.activation(
                            f32(XCS + par * 1024, 512), bank[:, :], AF.Exp,
                            bias=f32(bc + ft, 1), scale=1.0), inc="act")
                        a_exp = cnt["act"]
                        W("vector", "act", a_exp)
                        OP("vector", lambda e, bank=bank, bc=bc, ft=ft, par=par: e.tensor_scalar(
                            f32(XCS + par * 1024 + 512, 512), bank[:, :],
                            f32(bc + ft, 1), 0.0, ALU.add, ALU.max), inc="dve")
                        W("vector", "dve", cnt["dve"])
                        OP("vector", lambda e, dst=dst, ft=ft, par=par: e.scalar_tensor_tensor(
                            bf(dst + ft * 512, 512), f32(XCS + par * 1024, 512), 1.0,
                            f32(XCS + par * 1024 + 512, 512), ALU.min, ALU.add), inc="dve")
                        evac_act[bidx] = a_exp
                        evac_dve[bidx] = cnt["dve"]
                        elu_dve[par] = cnt["dve"]
                for j in range(4):
                    for half in range(2):
                        bidx = PROJ_BANKS[bi % 4]
                        bi += 1
                        bank = PB[bidx]
                        if bidx in evac_dve:
                            W("tensor", "dve", evac_dve[bidx])
                        if bidx in evac_act:
                            W("tensor", "act", evac_act[bidx])
                        for ki in range(ND):
                            OP("tensor", lambda e, ki=ki, j=j, half=half, bank=bank: e.matmul(
                                bank[:, :], bf(LN2T + ki * 512 + j * 128, 128),
                                bf(WVT + ki * 1024 + half * 512, 512),
                                start=(ki == 0), stop=(ki == ND - 1)))
                        OP("tensor", lambda e: e.matmul(
                            PB[6][:, 0:512], iden[:, :], bf(LN2T, 512),
                            start=True, stop=True), inc="pe")
                        v_pe = cnt["pe"]
                        W("vector", "pe", v_pe)
                        OP("vector", lambda e, j=j, half=half, bank=bank: e.scalar_tensor_tensor(
                            bf(VSB + j * 1024 + half * 512, 512), bank[:, :], 0.0,
                            lstg[:, half * 512:(half + 1) * 512], ALU.bypass, ALU.add),
                           inc="dve")
                        evac_dve[bidx] = cnt["dve"]
                proj_dve = cnt["dve"]
                projend[b] = cnt["pe"]
                if PARANOIA >= 2:
                    BAR()

                ABANKS = (PB[0], PB[1], PB[4], PB[5])
                for cc in range(NCH):
                    cg = b * NCH + cc
                    W("tensor", "dve", proj_dve if cc == 0 else core_dve_end)
                    for ft in range(ND):
                        OP("tensor", lambda e, ft=ft, cc=cc: e.transpose(
                            PT[:, ft * 128:(ft + 1) * 128],
                            bf(KF + ft * 512 + cc * 128, 128), iden[:]),
                           inc="pe" if ft == ND - 1 else None)
                    ktr_pe = cnt["pe"]
                    if PARANOIA >= 2:
                        BAR()

                    atev = {}
                    at_pe = {}
                    for h in range(HEADS):
                        bank = PB[h % 2]
                        hb = (h % 2) * 64
                        if h >= 2:
                            W("tensor", "dve", atev[h - 2])
                        OP("tensor", lambda e, bank=bank, h=h, cc=cc, hb=hb: e.matmul(
                            bank[:, 0:128],
                            bf(KF + (h // 2) * 512 + cc * 128, 128)[hb:hb + 64, :],
                            bf(QF + (h // 2) * 512 + cc * 128, 128)[hb:hb + 64, :],
                            start=True, stop=True), inc="pe")
                        at_pe[h] = cnt["pe"]
                        if h == 1:
                            W("vector", "pe", at_pe[1])
                            OP("vector", lambda e: e.tensor_copy(bf(KTM, 1024), PT[:, :]),
                               inc="dve")
                            ktm_done = cnt["dve"]
                        if h >= 1:
                            W("vector", "pe", at_pe[h])
                            OP("vector", lambda e, hp=h - 1, bp=PB[(h - 1) % 2]: e.scalar_tensor_tensor(
                                bf(ATS + hp * 128, 128), bp[:, 0:128], 0.0, mask[:],
                                ALU.bypass, ALU.mult), inc="dve")
                            atev[h - 1] = cnt["dve"]
                    W("tensor", "dve", ktm_done)
                    for h in range(HEADS):
                        hb = (h % 2) * 64
                        sbank, scol = (PB[2], h * 64) if h < 8 else (PB[3], (h - 8) * 64)
                        OP("tensor", lambda e, sbank=sbank, scol=scol, h=h, cc=cc, cg=cg, hb=hb: e.matmul(
                            sbank[hb:hb + 64, scol:scol + 64],
                            bf(KTM + h * 64, 64), bf(VSB + cc * 1024 + h * 64, 64),
                            start=False,
                            stop=(cg == NBLK * NCH - 1 and h % 8 == 7),
                            tile_position=(0, hb)),
                           inc="pe" if h == HEADS - 1 else None)
                    s_pe = cnt["pe"]
                    W("vector", "pe", s_pe)
                    OP("vector", lambda e, hp=HEADS - 1, bp=PB[(HEADS - 1) % 2]: e.scalar_tensor_tensor(
                        bf(ATS + hp * 128, 128), bp[:, 0:128], 0.0, mask[:],
                        ALU.bypass, ALU.mult), inc="dve")
                    atev[HEADS - 1] = cnt["dve"]
                    if PARANOIA >= 2:
                        BAR()
                    OP("tensor", lambda e: e.matmul(PB[6][:, 0:32], zer[:, :],
                                                    bf(LNT, 32), start=True, stop=True))
                    for h in range(HEADS):
                        hb = (h % 2) * 64
                        OP("tensor", lambda e, h=h, hb=hb: e.matmul(
                            PB[6][hb:hb + 64, 16 + h:17 + h], bf(KTM + h * 64, 64),
                            ones_c[:, 0:1], start=False, stop=False,
                            tile_position=(0, hb)))
                    for h in range(HEADS):
                        hb = (h % 2) * 64
                        OP("tensor", lambda e, h=h, cc=cc, hb=hb: e.matmul(
                            PB[6][:, h:h + 1],
                            bf(QF + (h // 2) * 512 + cc * 128, 128)[hb:hb + 64, :],
                            bf(ZSB, 16)[hb:hb + 64, h:h + 1], start=False, stop=False))
                    W("tensor", "dve", atev[HEADS - 1])
                    for h in range(HEADS):
                        OP("tensor", lambda e, h=h: e.matmul(
                            PB[6][:, h:h + 1], bf(ATS + h * 128, 128), ones_c[:, 0:1],
                            start=False, stop=(h == HEADS - 1)))
                    OP("tensor", lambda e: e.matmul(
                        PB[0][:, 0:512], iden[:, :], bf(QF, 512),
                        start=True, stop=True), inc="pe")
                    passa_pe = cnt["pe"]

                    W("vector", "pe", passa_pe)
                    OP("vector", lambda e: e.tensor_single_scalar(
                        f32(DCL, 16), PB[6][:, 0:16], ATT_EPS, ALU.max), inc="dve")
                    dcl_done = cnt["dve"]
                    W("scalar", "dve", dcl_done)
                    OP("scalar", lambda e: e.activation(f32(DTOT, 16), f32(DCL, 16), AF.Ln),
                       inc="act")
                    W("scalar", "act", cnt["act"])
                    OP("scalar", lambda e: e.activation(f32(REC, 16), f32(DTOT, 16), AF.Exp,
                                                        scale=-1.0), inc="act")
                    rec_done = cnt["act"]
                    OP("vector", lambda e: e.tensor_tensor(
                        f32(ZF, 16), f32(ZF, 16), PB[6][:, 16:32],
                        ALU.add), inc="dve")
                    zacc_done = cnt["dve"]

                    if PARANOIA >= 2:
                        BAR()
                    W("vector", "act", rec_done)
                    nev = {}
                    n_pe = {}
                    for h in range(HEADS):
                        bank = PB[4] if h % 2 == 0 else PB[5]
                        hb = (h % 2) * 64
                        if h >= 2:
                            W("tensor", "dve", nev[h - 2])
                        OP("tensor", lambda e, bank=bank, h=h, cc=cc: e.matmul(
                            bank[:, 0:64], bf(ATS + h * 128, 128),
                            bf(VSB + cc * 1024 + h * 64, 64), start=True, stop=False))
                        OP("tensor", lambda e, bank=bank, h=h, cc=cc, hb=hb: e.matmul(
                            bank[:, 0:64],
                            bf(QF + (h // 2) * 512 + cc * 128, 128)[hb:hb + 64, :],
                            bf(SSB + h * 64, 64)[hb:hb + 64, :], start=False, stop=True), inc="pe")
                        n_pe[h] = cnt["pe"]
                        if h >= 1:
                            W("vector", "pe", n_pe[h])
                            OP("vector", lambda e, hp=h - 1, bp=PB[4] if (h - 1) % 2 == 0 else PB[5], cc=cc: e.tensor_scalar(
                                bf(OTM + cc * 1024 + hp * 64, 64), bp[:, 0:64],
                                f32(REC + hp, 1), None, ALU.mult), inc="dve")
                            nev[h - 1] = cnt["dve"]
                    OP("tensor", lambda e: e.matmul(
                        PB[0][:, 0:512], iden[:, :], bf(QF, 512),
                        start=True, stop=True), inc="pe")
                    passb_pe = cnt["pe"]
                    W("vector", "pe", passb_pe)
                    OP("vector", lambda e, hp=HEADS - 1, cc=cc: e.tensor_scalar(
                        bf(OTM + cc * 1024 + hp * 64, 64), PB[5][:, 0:64],
                        f32(REC + hp, 1), None, ALU.mult), inc="dve")
                    nev[HEADS - 1] = cnt["dve"]

                    W("vector", "pe", passb_pe)
                    OP("vector", lambda e: e.tensor_copy(bf(SSB, 512), PB[2][:, :]),
                       inc="dve")
                    OP("vector", lambda e: e.tensor_copy(bf(SSB + 512, 512), PB[3][:, :]),
                       inc="dve")
                    OP("vector", lambda e: e.tensor_copy(bf(ZSB, 16), f32(ZF, 16)),
                       inc="dve")
                    core_dve_end = cnt["dve"]
                    W("tensor", "dve", max(zacc_done, core_dve_end))
                    if PARANOIA >= 2:
                        BAR()

                oev = {}
                for j in range(4):
                    W("tensor", "dve", core_dve_end if j == 0 else oev[j - 1])
                    for ft in range(ND):
                        OP("tensor", lambda e, j=j, ft=ft: e.transpose(
                            PT[:, ft * 128:(ft + 1) * 128],
                            bf(OTM + j * 1024 + ft * 128, 128), iden[:]))
                    OP("tensor", lambda e: e.matmul(
                        PB[6][:, 0:512], iden[:, :], bf(OTM, 512),
                        start=True, stop=True), inc="pe")
                    otr_pe = cnt["pe"]
                    W("vector", "pe", otr_pe)
                    OP("vector", lambda e, j=j: e.tensor_copy(
                        ABF[:, OFM:OFM + 4096].rearrange("p (f t) -> p f t", f=8)[:, :, j * 128:(j + 1) * 128],
                        PT[:, :].rearrange("p (f t) -> p f t", f=8)), inc="dve")
                    oev[j] = cnt["dve"]
                ofm_done = cnt["dve"]
                wo_stt_ms = {}
                W("tensor", "dve", ofm_done)
                for j in range(4):
                    t = b * 4 + j
                    for half in range(2):
                        bkey = (j * 2 + half) % 2
                        bank = PB[bkey]
                        if bkey in wo_stt_ms:
                            W("tensor", "dve", wo_stt_ms[bkey])
                        for ki in range(ND):
                            OP("tensor", lambda e, ki=ki, j=j, half=half, bank=bank: e.matmul(
                                bank[:, :], bf(OFM + ki * 512 + j * 128, 128),
                                bf(WOT + ki * 1024 + half * 512, 512),
                                start=(ki == 0), stop=(ki == ND - 1)))
                        OP("tensor", lambda e: e.matmul(
                            PB[6][:, 0:512], iden[:, :], bf(OFM, 512),
                            start=True, stop=True), inc="pe")
                        w_pe = cnt["pe"]
                        W("vector", "pe", w_pe)
                        OP("vector", lambda e, t=t, half=half, bank=bank: e.scalar_tensor_tensor(
                            f32(R0 + t * D + half * 512, 512), bank[:, :], 0.0,
                            f32(R0 + t * D + half * 512, 512), ALU.bypass, ALU.add),
                           inc="dve")
                        W("vector", "dve", cnt["dve"])
                        OP("vector", lambda e, t=t, half=half: e.tensor_tensor(
                            f32(R0 + t * D + half * 512, 512),
                            f32(R0 + t * D + half * 512, 512),
                            lstg[:, 1024 + half * 512:1024 + (half + 1) * 512], ALU.add),
                           inc="dve")
                        wo_stt_ms[bkey] = cnt["dve"]
                prev_block_dve = cnt["dve"]
                if PARANOIA >= 2:
                    BAR()
            if DBG_ATTN:
                dbgwq = nc.dram_tensor("dbgwq", [128, 8192], BF16, kind="ExternalOutput").ap()
                dbgwk = nc.dram_tensor("dbgwk", [128, 8192], BF16, kind="ExternalOutput").ap()
                dbgwv = nc.dram_tensor("dbgwv", [128, 8192], BF16, kind="ExternalOutput").ap()
                dbgwo = nc.dram_tensor("dbgwo", [128, 8192], BF16, kind="ExternalOutput").ap()
                dbgln = nc.dram_tensor("dbgln2", [T, D], BF16, kind="ExternalOutput").ap()
                dbgbq = nc.dram_tensor("dbgbq", [128, 16], F32, kind="ExternalOutput").ap()
                W("sync", "dve", cnt["dve"])
                W("sync", "pe", cnt["pe"])
                OP("sync", lambda e: e.dma_start(dbgwq, bf(WQT, 8192)), inc="dls", n=16)
                OP("sync", lambda e: e.dma_start(dbgwk, bf(WKT, 8192)), inc="dls", n=16)
                OP("sync", lambda e: e.dma_start(dbgwv, bf(WVT, 8192)), inc="dls", n=16)
                OP("sync", lambda e: e.dma_start(dbgwo, bf(WOT, 8192)), inc="dls", n=16)
                OP("sync", lambda e: e.dma_start(dbgln, lnd), inc="dls", n=16)
                OP("sync", lambda e: e.dma_start(dbgbq, f32(BQC, 16)), inc="dls", n=16)
                dbgdcl = nc.dram_tensor("dbgdcl", [128, 16], F32, kind="ExternalOutput").ap()
                dbgrec = nc.dram_tensor("dbgrec", [128, 16], F32, kind="ExternalOutput").ap()
                dbgl2 = nc.dram_tensor("dbgl2", [128, 4096], BF16, kind="ExternalOutput").ap()
                dbgofm = nc.dram_tensor("dbgofm", [128, 4096], BF16, kind="ExternalOutput").ap()
                OP("sync", lambda e: e.dma_start(dbgofm, bf(OFM, 4096)), inc="dls", n=16)
                OP("sync", lambda e: e.dma_start(dbgdcl, f32(DCL, 16)), inc="dls", n=16)
                OP("sync", lambda e: e.dma_start(dbgrec, f32(REC, 16)), inc="dls", n=16)
                OP("sync", lambda e: e.dma_start(dbgl2, bf(LN2T, 4096)), inc="dls", n=16)
                dbgs = nc.dram_tensor("dbgs", [128, 1024], BF16, kind="ExternalOutput").ap()
                dbgz = nc.dram_tensor("dbgz", [128, 16], BF16, kind="ExternalOutput").ap()
                dbgq = nc.dram_tensor("dbgq", [128, 4096], BF16, kind="ExternalOutput").ap()
                dbgk = nc.dram_tensor("dbgk", [128, 4096], BF16, kind="ExternalOutput").ap()
                dbgv = nc.dram_tensor("dbgv", [128, 4096], BF16, kind="ExternalOutput").ap()
                dbgat = nc.dram_tensor("dbgat", [128, 2048], BF16, kind="ExternalOutput").ap()
                dbgo = nc.dram_tensor("dbgo", [128, 4096], BF16, kind="ExternalOutput").ap()
                W("sync", "dve", cnt["dve"])
                W("sync", "pe", cnt["pe"])
                for nm, src in (("s", (SSB, 1024)), ("z", (ZSB, 16)), ("q", (QF, 4096)),
                                ("k", (KF, 4096)), ("v", (VSB, 4096)), ("at", (ATS, 2048)),
                                ("o", (OTM, 4096))):
                    dst = {"s": dbgs, "z": dbgz, "q": dbgq, "k": dbgk, "v": dbgv,
                           "at": dbgat, "o": dbgo}[nm]
                    OP("sync", lambda e, d=dst, s=src: e.dma_start(d, bf(*s)), inc="dls", n=16)
            return cnt["dve"], cnt["pe"], cnt["act"]

        if DBG_LN:
            dbgln = nc.dram_tensor("dbgln", [T, D], BF16, kind="ExternalOutput").ap()
        r_dve, r_pe, r_act = 0, 0, bias_done
        if "ffn1" in PHASES:
            ln1 = emit_ln()
            if DBG_LN:
                W("sync", "dls", ln1)
                OP("sync", lambda e: e.dma_start(dbgln, lnd), inc="dls", n=16)
            r_dve, r_pe, r_act = emit_ffn(ln1, first=True, pre_act=bias_done)
        if "attn" in PHASES:
            ln2 = emit_ln(wait_dve=r_dve, wait_pe=r_pe, wait_act=r_act)
            r_dve, r_pe, r_act = emit_attn(ln2, r_pe, r_dve, r_act)
        if "ffn2" in PHASES:
            ln3 = emit_ln(wait_dve=r_dve, wait_pe=r_pe, wait_act=r_act)
            r_dve, r_pe, r_act = emit_ffn(ln3, first=False, pre_pe=r_pe,
                                          pre_dve=r_dve, pre_act=r_act)

        W("sync", "dve", r_dve)
        W("sync", "pe", r_pe)
        OP("sync", lambda e: e.dma_start(
            out.rearrange("(t p) d -> p t d", p=128),
            f32(R0, 16384).rearrange("p (t d) -> p t d", t=NT)), inc="dls", n=16)
        W("sync", "dls", cnt["dls"])

        @block.sync
        def _(e):
            for f in ops["sync"]:
                f(e)

        @block.gpsimd
        def _(e):
            for f in ops["gpsimd"]:
                f(e)

        @block.vector
        def _(e):
            for f in ops["vector"]:
                f(e)

        @block.scalar
        def _(e):
            for f in ops["scalar"]:
                f(e)

        @block.tensor
        def _(e):
            for f in ops["tensor"]:
                f(e)

    return nc


_NC_CACHE = {}
_LAST_RES = None


def kernel(**inputs):
    key = tuple(PHASES)
    if key not in _NC_CACHE:
        _NC_CACHE[key] = build_nc()
    nc = _NC_CACHE[key]

    x = np.ascontiguousarray(np.asarray(inputs["L0_attn_x"], dtype=np.float32))
    base = {
        "wq": np.ascontiguousarray(np.asarray(inputs["L0_attn_Wq"], dtype=np.float32)),
        "wk": np.ascontiguousarray(np.asarray(inputs["L0_attn_Wk"], dtype=np.float32)),
        "wv": np.ascontiguousarray(np.asarray(inputs["L0_attn_Wv"], dtype=np.float32)),
        "wo": np.ascontiguousarray(np.asarray(inputs["L0_attn_Wo"], dtype=np.float32)),
        "bq": np.ascontiguousarray(np.asarray(inputs["L0_attn_bq"], dtype=np.float32)),
        "bk": np.ascontiguousarray(np.asarray(inputs["L0_attn_bk"], dtype=np.float32)),
        "bv": np.ascontiguousarray(np.asarray(inputs["L0_attn_bv"], dtype=np.float32)),
        "bo": np.ascontiguousarray(np.asarray(inputs["L0_attn_bo"], dtype=np.float32)),
        "wg": np.ascontiguousarray(np.asarray(inputs["L0_ffn_w_gate"], dtype=np.int32)),
        "wu": np.ascontiguousarray(np.asarray(inputs["L0_ffn_w_up"], dtype=np.int32)),
        "wd": np.ascontiguousarray(np.asarray(inputs["L0_ffn_w_down"], dtype=np.int32)),
    }
    in_maps = [dict(base, x=np.ascontiguousarray(x[i])) for i in range(8)]
    res = run_bass_kernel_spmd(nc, in_maps, core_ids=list(range(8)))
    global _LAST_RES
    _LAST_RES = res
    return np.stack([np.asarray(res.results[i]["out"], dtype=np.float32) for i in range(8)])


# version 05
# speedup vs baseline: 1.0166x; 1.0166x over previous
import numpy as np
import ml_dtypes
from contextlib import ExitStack

import concourse.bass as bass
import concourse.mybir as mybir
from concourse.bass_utils import run_bass_kernel_spmd

F32 = mybir.dt.float32
BF16 = mybir.dt.bfloat16
I32 = mybir.dt.int32
AF = mybir.ActivationFunctionType
ALU = mybir.AluOpType
AX = mybir.AxisListType

T = 2048
D = 1024
HFF = 3072
NT = T // 128
ND = D // 128
HEADS = 16
DH = 64
CH = 128
BLK = 512
NBLK = T // BLK
NCH = BLK // CH
HC = 512
NHC = HFF // HC
LN_EPS = 1e-5
ATT_EPS = 1e-6
DOWN_SCALE = 0.5 / (127.0 * 127.0)
SILU_SCALE = 1.0 / 127.0

PHASES = ("ffn1", "attn", "ffn2")
DBG_LN = False
PARANOIA = 1
DBG_ATTN = False

R0 = 0
XCS = 16384
SUMS = 18432
NEGM = 18448
VARS = 18464
RSTD = 18480
DSB = 18496
DTOT = 18512
DCL = 18528
REC = 18544
ZF = 18560
BQC = 18576
BKC = 18584
EPSC = 18592
NF32 = 18600

SIL0, SIL1 = 0, 512
LNT = 1024
WG0, WG1 = 17408, 21504
WU0, WU1 = 25600, 29696
WD0, WD1 = 33792, 37888
H0, H1 = 41984, 44032
FFN_END = 46080
LN2T = 0
QF = 4096
KF = 8192
KTM = 12288
VSB = 13312
OTM = 17408
OFM = 21504
ATS = 25600
SSB = 27648
ZSB = 28672
WQT = 28688
WKT = 36880
WVT = 45072
WOT = 53264
NBF = 61456


def build_nc():
    nc = bass.Bass("TRN2", target_bir_lowering=False)

    x_in = nc.dram_tensor("x", [T, D], F32, kind="ExternalInput").ap()
    wq = nc.dram_tensor("wq", [D, D], F32, kind="ExternalInput").ap()
    wk = nc.dram_tensor("wk", [D, D], F32, kind="ExternalInput").ap()
    wv = nc.dram_tensor("wv", [D, D], F32, kind="ExternalInput").ap()
    wo = nc.dram_tensor("wo", [D, D], F32, kind="ExternalInput").ap()
    bq = nc.dram_tensor("bq", [D], F32, kind="ExternalInput").ap()
    bk = nc.dram_tensor("bk", [D], F32, kind="ExternalInput").ap()
    bv = nc.dram_tensor("bv", [D], F32, kind="ExternalInput").ap()
    bo = nc.dram_tensor("bo", [D], F32, kind="ExternalInput").ap()
    wg = nc.dram_tensor("wg", [HFF, D], I32, kind="ExternalInput").ap()
    wu = nc.dram_tensor("wu", [HFF, D], I32, kind="ExternalInput").ap()
    wd = nc.dram_tensor("wd", [D, HFF], I32, kind="ExternalInput").ap()
    out = nc.dram_tensor("out", [T, D], F32, kind="ExternalOutput").ap()

    wgc = nc.dram_tensor("wgc", [HFF, D], BF16).ap()
    wuc = nc.dram_tensor("wuc", [HFF, D], BF16).ap()
    wdc = nc.dram_tensor("wdc", [D, HFF], BF16).ap()
    wqc = nc.dram_tensor("wqc", [D, D], BF16).ap()
    wkc = nc.dram_tensor("wkc", [D, D], BF16).ap()
    wvc = nc.dram_tensor("wvc", [D, D], BF16).ap()
    woc = nc.dram_tensor("woc", [D, D], BF16).ap()
    lnd = nc.dram_tensor("lnd", [T, D], BF16).ap()

    with ExitStack() as stk:
        E = stk.enter_context
        AF32 = E(nc.sbuf_tensor("AF32", [128, NF32], F32))
        ABF = E(nc.sbuf_tensor("ABF", [128, NBF], BF16))
        mask = E(nc.sbuf_tensor("mask", [128, 128], BF16))
        iden = E(nc.sbuf_tensor("iden", [128, 128], BF16))
        iden8 = E(nc.sbuf_tensor("iden8", [8, 8], F32))
        iotas = E(nc.sbuf_tensor("iotas", [128, 128], F32))
        ones_l = E(nc.sbuf_tensor("ones_l", [1, 128], BF16))
        ones_c = E(nc.sbuf_tensor("ones_c", [128, 1], BF16))
        b8a = E(nc.sbuf_tensor("b8a", [8, 128], F32))
        b8b = E(nc.sbuf_tensor("b8b", [8, 128], F32))
        lstg = E(nc.sbuf_tensor("lstg", [128, 2048], BF16))
        zer = E(nc.sbuf_tensor("zer", [128, 128], BF16))
        PB = [E(nc.psum_tensor(f"PB{i}", [128, 512], F32)) for i in range(7)]
        PT = E(nc.psum_tensor("PT", [128, 1024], BF16))
        sem = {s: E(nc.semaphore(s)) for s in ("pe", "dve", "act", "pool", "dls", "dca")}
        block = E(nc.Block())

        ops = {k: [] for k in ("sync", "gpsimd", "vector", "scalar", "tensor")}
        cnt = {k: 0 for k in sem}

        def OP(eng, builder, inc=None, n=1):
            if inc is not None:
                cnt[inc] += n
            def run(e, b=builder, s=inc, m=n):
                inst = b(e)
                if s is not None:
                    inst.then_inc(sem[s], m)
            ops[eng].append(run)
            return cnt[inc] if inc is not None else None

        def W(eng, s, v):
            assert v <= cnt[s], f"wait on future value: {s} {v} > {cnt[s]}"
            if v <= 0:
                return
            ops[eng].append(lambda e, s=s, v=v: e.wait_ge(sem[s], v))

        def BAR():
            for eng in ("tensor", "vector", "scalar", "sync", "gpsimd"):
                for s in ("pe", "dve", "act", "dls", "dca", "pool"):
                    if cnt[s] > 0:
                        W(eng, s, cnt[s])

        def f32(c0, w):
            return AF32[:, c0:c0 + w]

        def bf(c0, w):
            return ABF[:, c0:c0 + w]

        OP("gpsimd", lambda e: e.iota(iotas[:], [[1, 128]], channel_multiplier=-1,
                                      allow_small_or_imprecise_dtypes=True), inc="pool")
        OP("gpsimd", lambda e: e.memset(ones_l[:], 1.0))
        OP("gpsimd", lambda e: e.memset(ones_c[:], 1.0))
        OP("gpsimd", lambda e: e.memset(zer[:], 0.0))
        OP("gpsimd", lambda e: e.memset(f32(ZF, 16), 0.0))
        OP("gpsimd", lambda e: e.memset(bf(SSB, 1024), 0.0))
        OP("gpsimd", lambda e: e.memset(bf(ZSB, 16), 0.0))
        OP("gpsimd", lambda e: e.memset(f32(EPSC, 1), LN_EPS), inc="pool")
        W("vector", "pool", 1)
        OP("vector", lambda e: e.tensor_single_scalar(mask[:], iotas[:], 0.0, ALU.is_ge))
        OP("vector", lambda e: e.tensor_single_scalar(iden[:], iotas[:], 0.0, ALU.is_equal))
        OP("vector", lambda e: e.tensor_single_scalar(iden8[:], iotas[0:8, 0:8], 0.0, ALU.is_equal),
           inc="dve")
        consts_dve = cnt["dve"]
        for eng in ("tensor", "vector", "scalar"):
            W(eng, "pool", 2)
        W("tensor", "dve", consts_dve)
        W("scalar", "dve", consts_dve)

        cast_ms = {}
        OP("gpsimd", lambda e: e.dma_start(wdc, wd), inc="dca", n=16)
        for c in range(NHC):
            OP("gpsimd", lambda e, c=c: e.dma_start(wgc[c * HC:(c + 1) * HC, :],
                                                    wg[c * HC:(c + 1) * HC, :]), inc="dca", n=16)
            OP("gpsimd", lambda e, c=c: e.dma_start(wuc[c * HC:(c + 1) * HC, :],
                                                    wu[c * HC:(c + 1) * HC, :]), inc="dca", n=16)
            cast_ms[c] = cnt["dca"]
        for cdst, csrc in ((wqc, wq), (wkc, wk), (wvc, wv), (woc, wo)):
            OP("gpsimd", lambda e, a=cdst, b=csrc: e.dma_start(a, b), inc="dca", n=16)
        cast_ms["attn"] = cnt["dca"]

        x_ld = OP("sync", lambda e: e.dma_start(
            f32(R0, 16384).rearrange("p (t d) -> p t d", t=NT),
            x_in.rearrange("(t p) d -> p t d", p=128)), inc="dls", n=16)


        OP("sync", lambda e: e.dma_start(b8a[:], bq.rearrange("(a b) -> a b", a=8)),
           inc="dls", n=16)
        OP("sync", lambda e: e.dma_start(b8b[:], bk.rearrange("(a b) -> a b", a=8)),
           inc="dls", n=16)
        bqk_ld = cnt["dls"]
        W("tensor", "dls", bqk_ld)
        OP("tensor", lambda e: e.transpose(PB[0][0:128, 0:8], b8a[0:8, :], iden8[:]))
        tr1 = OP("tensor", lambda e: e.matmul(
            PB[1][:, 0:128], iden[:, :], mask[:, 0:128],
            start=True, stop=True), inc="pe")
        W("scalar", "pe", tr1)
        cp1 = OP("scalar", lambda e: e.copy(f32(BQC, 8), PB[0][0:128, 0:8]), inc="act")
        W("tensor", "act", cp1)
        OP("tensor", lambda e: e.transpose(PB[0][0:128, 0:8], b8b[0:8, :], iden8[:]))
        tr2 = OP("tensor", lambda e: e.matmul(
            PB[1][:, 0:128], iden[:, :], mask[:, 0:128],
            start=True, stop=True), inc="pe")
        W("scalar", "pe", tr2)
        bias_done = OP("scalar", lambda e: e.copy(f32(BKC, 8), PB[0][0:128, 0:8]), inc="act")

        def emit_ln(wait_dve=0, wait_pe=0, wait_act=0):
            if PARANOIA:
                BAR()
            ln_in_ld = cnt["dls"] if cnt["dls"] > x_ld else x_ld
            """LN(r) for all 16 tiles -> lnd (DRAM, token-major bf16).
            ln = r*rstd + (negm*rstd) via a single two-op tensor_scalar."""
            W("vector", "dls", ln_in_ld)
            if wait_pe:
                W("vector", "pe", wait_pe)
            if wait_dve:
                W("vector", "dve", wait_dve)
            for i in range(NT):
                OP("vector", lambda e, i=i: e.reduce_sum(
                    f32(SUMS + i, 1), f32(R0 + i * D, D), axis=AX.X), inc="dve")
            sums_done = cnt["dve"]
            if wait_act:
                W("scalar", "act", wait_act)
            W("scalar", "dve", sums_done)
            OP("scalar", lambda e: e.mul(f32(NEGM, 16), f32(SUMS, 16), -1.0 / D), inc="act")
            W("scalar", "act", cnt["act"])
            sq_ms = {}
            for i in range(NT):
                par = i % 2
                if i - 2 in sq_ms:
                    W("scalar", "dve", sq_ms[i - 2])
                OP("scalar", lambda e, i=i, par=par: e.activation(
                    lstg[:, par * 1024:(par + 1) * 1024], f32(R0 + i * D, D), AF.Square,
                    bias=f32(NEGM + i, 1), scale=1.0), inc="act")
                a = cnt["act"]
                W("vector", "act", a)
                OP("vector", lambda e, par=par, i=i: e.reduce_sum(
                    f32(VARS + i, 1), lstg[:, par * 1024:(par + 1) * 1024], axis=AX.X),
                   inc="dve")
                sq_ms[i] = cnt["dve"]
            W("scalar", "dve", cnt["dve"])
            OP("scalar", lambda e: e.activation(f32(DCL, 16), f32(VARS, 16), AF.Ln,
                                                bias=f32(EPSC, 1), scale=1.0 / D), inc="act")
            W("scalar", "act", cnt["act"])
            OP("scalar", lambda e: e.activation(f32(RSTD, 16), f32(DCL, 16), AF.Exp,
                                                scale=-0.5), inc="act")
            rstd_done = cnt["act"]
            W("vector", "act", rstd_done)
            OP("vector", lambda e: e.tensor_tensor(f32(NEGM, 16), f32(NEGM, 16),
                                                   f32(RSTD, 16), ALU.mult), inc="dve")
            W("vector", "dve", cnt["dve"])
            st_ms = {}
            for i in range(NT):
                par = i % 2
                if i - 2 in st_ms:
                    W("vector", "dls", st_ms[i - 2])
                OP("vector", lambda e, i=i, par=par: e.tensor_scalar(
                    lstg[:, par * 1024:(par + 1) * 1024], f32(R0 + i * D, D),
                    f32(RSTD + i, 1), f32(NEGM + i, 1), ALU.mult, ALU.add), inc="dve")
                a = cnt["dve"]
                W("sync", "dve", a)
                OP("sync", lambda e, i=i, par=par: e.dma_start(
                    lnd[i * 128:(i + 1) * 128, :], lstg[:, par * 1024:(par + 1) * 1024]),
                    inc="dls", n=16)
                st_ms[i] = cnt["dls"]
            return cnt["dls"]

        def emit_ffn(ln_stores_done, first, pre_pe=0, pre_dve=0, pre_act=0):
            if PARANOIA:
                BAR()
            if pre_pe:
                W("sync", "pe", pre_pe)
            if pre_dve:
                W("sync", "dve", pre_dve)
            W("sync", "dls", ln_stores_done)
            for ki in range(ND):
                OP("sync", lambda e, ki=ki: e.dma_start(
                    bf(LNT + ki * 2048, 2048), lnd[:, ki * 128:(ki + 1) * 128],
                    transpose=True), inc="dls", n=16)
            lnt_done = cnt["dls"]

            wbg, wbu, wbd = (WG0, WG1), (WU0, WU1), (WD0, WD1)
            ld_ms, pe_ms = {}, {}

            def emit_wload(c):
                par = c % 2
                if first:
                    W("sync", "dca", cast_ms[c])
                if c - 2 in pe_ms:
                    W("sync", "pe", pe_ms[c - 2])
                for ki in range(ND):
                    OP("sync", lambda e, c=c, ki=ki, par=par: e.dma_start(
                        bf(wbg[par] + ki * 512, 512),
                        wgc[c * HC:(c + 1) * HC, ki * 128:(ki + 1) * 128],
                        transpose=True), inc="dls", n=16)
                for ki in range(ND):
                    OP("sync", lambda e, c=c, ki=ki, par=par: e.dma_start(
                        bf(wbu[par] + ki * 512, 512),
                        wuc[c * HC:(c + 1) * HC, ki * 128:(ki + 1) * 128],
                        transpose=True), inc="dls", n=16)
                for ht in range(4):
                    OP("sync", lambda e, c=c, ht=ht, par=par: e.dma_start(
                        bf(wbd[par] + ht * 1024, 1024),
                        wdc[:, c * HC + ht * 128:c * HC + (ht + 1) * 128],
                        transpose=True), inc="dls", n=16)
                ld_ms[c] = cnt["dls"]

            emit_wload(0)
            silu_ms, h_ms, stt_ms = {}, {}, {}
            first_mm = True
            for c in range(NHC):
                par = c % 2
                if c + 1 < NHC:
                    emit_wload(c + 1)
                W("tensor", "dls", max(ld_ms[c], lnt_done))
                if first_mm:
                    if pre_act:
                        W("tensor", "act", pre_act)
                    if pre_dve:
                        W("tensor", "dve", pre_dve)
                    first_mm = False
                for cc in range(4):
                    for ht in range(4):
                        gbank = PB[0] if ht % 2 == 0 else PB[4]
                        ubank = PB[1] if ht % 2 == 0 else PB[5]
                        key = (c, cc, ht)
                        prev = (c, cc, ht - 2) if ht >= 2 else (
                            (c, cc - 1, ht + 2) if cc else (c - 1, 3, ht + 2))
                        if prev in silu_ms:
                            W("tensor", "act", silu_ms[prev])
                        if prev in h_ms:
                            W("tensor", "dve", h_ms[prev])
                        for ki in range(ND):
                            OP("tensor", lambda e, par=par, ki=ki, ht=ht, cc=cc, g=gbank: e.matmul(
                                g[:, :], bf(wbg[par] + ki * 512 + ht * 128, 128),
                                bf(LNT + ki * 2048 + cc * 512, 512),
                                start=(ki == 0), stop=(ki == ND - 1)))
                        for ki in range(ND):
                            OP("tensor", lambda e, par=par, ki=ki, ht=ht, cc=cc, u=ubank: e.matmul(
                                u[:, :], bf(wbu[par] + ki * 512 + ht * 128, 128),
                                bf(LNT + ki * 2048 + cc * 512, 512),
                                start=(ki == 0), stop=(ki == ND - 1)))
                        OP("tensor", lambda e: e.matmul(
                            PB[6][:, 0:512], iden[:, :], bf(LNT, 512),
                            start=True, stop=True), inc="pe")
                        u_pe = cnt["pe"]
                        W("scalar", "pe", u_pe)
                        if prev in h_ms:
                            W("scalar", "dve", h_ms[prev])
                        OP("scalar", lambda e, ht=ht, g=gbank: e.activation(
                            bf(SIL0 + (ht % 2) * 512, 512), g[:, :], AF.Silu,
                            scale=SILU_SCALE), inc="act")
                        silu_ms[key] = cnt["act"]
                        W("vector", "act", silu_ms[key])
                        W("vector", "pe", u_pe)
                        OP("vector", lambda e, ht=ht, cc=cc, u=ubank: e.tensor_tensor(
                            bf((H0 if cc % 2 == 0 else H1) + ht * 512, 512),
                            bf(SIL0 + (ht % 2) * 512, 512), u[:, :], ALU.mult), inc="dve")
                        h_ms[key] = cnt["dve"]
                    hdone = cnt["dve"]
                    W("tensor", "dve", hdone)
                    for j in range(4):
                        t = cc * 4 + j
                        for half in range(2):
                            bkey = (j * 2 + half) % 2
                            dbank = PB[2] if bkey == 0 else PB[3]
                            if bkey in stt_ms:
                                W("tensor", "dve", stt_ms[bkey])
                            for ht in range(4):
                                OP("tensor", lambda e, ht=ht, cc=cc, j=j, half=half, par=par, d=dbank: e.matmul(
                                    d[:, :],
                                    bf((H0 if cc % 2 == 0 else H1) + ht * 512 + j * 128, 128),
                                    bf(wbd[par] + ht * 1024 + half * 512, 512),
                                    start=(ht == 0), stop=(ht == 3)))
                            OP("tensor", lambda e: e.matmul(
                                PB[6][:, 0:512], iden[:, :], bf(LNT, 512),
                                start=True, stop=True), inc="pe")
                            d_pe = cnt["pe"]
                            W("vector", "pe", d_pe)
                            OP("vector", lambda e, t=t, half=half, d=dbank: e.scalar_tensor_tensor(
                                f32(R0 + t * D + half * 512, 512), d[:, :], DOWN_SCALE,
                                f32(R0 + t * D + half * 512, 512), ALU.mult, ALU.add),
                               inc="dve")
                            stt_ms[bkey] = cnt["dve"]
                pe_ms[c] = cnt["pe"]
            return cnt["dve"], cnt["pe"], cnt["act"]

        def emit_attn(ln_stores_done, pre_pe, pre_dve, pre_act):
            if PARANOIA:
                BAR()
            W("gpsimd", "dls", ln_stores_done)
            W("gpsimd", "pe", pre_pe)
            W("gpsimd", "dve", pre_dve)
            OP("gpsimd", lambda e: e.memset(f32(ZF, 16), 0.0))
            OP("gpsimd", lambda e: e.memset(bf(SSB, 1024), 0.0))
            OP("gpsimd", lambda e: e.memset(bf(ZSB, 16), 0.0), inc="pool")
            sz_ms = cnt["pool"]
            if pre_dve:
                W("tensor", "dve", pre_dve)
            if pre_act:
                W("tensor", "act", pre_act)
            OP("tensor", lambda e: e.matmul(PB[2][:, :], zer[:, :], bf(LNT, 512),
                                            start=True, stop=True))
            OP("tensor", lambda e: e.matmul(PB[3][:, :], zer[:, :], bf(LNT, 512),
                                            start=True, stop=True))
            OP("gpsimd", lambda e: e.dma_start(
                lstg[:, 0:1024], bass.AP(bv.tensor, 0, [[0, 128], [1, D]])), inc="dca", n=16)
            OP("gpsimd", lambda e: e.dma_start(
                lstg[:, 1024:2048], bass.AP(bo.tensor, 0, [[0, 128], [1, D]])), inc="dca", n=16)
            bias_full = cnt["dca"]
            W("sync", "dca", cast_ms["attn"])
            for ki in range(ND):
                OP("sync", lambda e, ki=ki: e.dma_start(
                    bf(WOT + ki * 1024, 1024), woc[:, ki * 128:(ki + 1) * 128],
                    transpose=True), inc="dls", n=16)
            W("sync", "pe", pre_pe)
            W("sync", "dve", pre_dve)
            for wdst, wsrc in ((WQT, wqc), (WKT, wkc), (WVT, wvc)):
                for ki in range(ND):
                    OP("sync", lambda e, ki=ki, wdst=wdst, wsrc=wsrc: e.dma_start(
                        bf(wdst + ki * 1024, 1024), wsrc[:, ki * 128:(ki + 1) * 128],
                        transpose=True), inc="dls", n=16)
            attw_done = cnt["dls"]

            PROJ_BANKS = (0, 1, 4, 5)
            projend = {}
            prev_block_dve = pre_dve
            core_dve_end = 0
            for b in range(NBLK):
                if b == 0:
                    W("sync", "dls", ln_stores_done)
                else:
                    W("sync", "pe", projend[b - 1])
                for ft in range(ND):
                    OP("sync", lambda e, b=b, ft=ft: e.dma_start(
                        bf(LN2T + ft * 512, 512),
                        lnd[b * BLK:(b + 1) * BLK, ft * 128:(ft + 1) * 128],
                        transpose=True), inc="dls", n=16)
                ln2_done = cnt["dls"]

                W("tensor", "dls", max(ln2_done, attw_done))
                W("tensor", "dve", prev_block_dve)
                if b == 0:
                    W("tensor", "act", max(bias_done, pre_act))
                    W("tensor", "pool", sz_ms)
                    W("vector", "pool", sz_ms)
                    W("vector", "dca", bias_full)
                evac_act = {}
                evac_dve = {}
                elu_dve = {}
                bi = 0
                for wt, bc, dst in ((WQT, BQC, QF), (WKT, BKC, KF)):
                    for ft in range(ND):
                        bidx = PROJ_BANKS[bi % 4]
                        par = bi % 2
                        bi += 1
                        bank = PB[bidx]
                        if bidx in evac_dve:
                            W("tensor", "dve", evac_dve[bidx])
                        for ki in range(ND):
                            OP("tensor", lambda e, wt=wt, ki=ki, ft=ft, bank=bank: e.matmul(
                                bank[:, :], bf(wt + ki * 1024 + ft * 128, 128),
                                bf(LN2T + ki * 512, 512),
                                start=(ki == 0), stop=(ki == ND - 1)))
                        OP("tensor", lambda e: e.matmul(
                            PB[6][:, 0:512], iden[:, :], bf(LN2T, 512),
                            start=True, stop=True), inc="pe")
                        p_pe = cnt["pe"]
                        W("scalar", "pe", p_pe)
                        if par in elu_dve:
                            W("scalar", "dve", elu_dve[par])
                        OP("scalar", lambda e, bank=bank, bc=bc, ft=ft, par=par: e.activation(
                            f32(XCS + par * 1024, 512), bank[:, :], AF.Exp,
                            bias=f32(bc + ft, 1), scale=1.0), inc="act")
                        a_exp = cnt["act"]
                        W("vector", "act", a_exp)
                        OP("vector", lambda e, bank=bank, bc=bc, ft=ft, par=par: e.tensor_scalar(
                            f32(XCS + par * 1024 + 512, 512), bank[:, :],
                            f32(bc + ft, 1), 0.0, ALU.add, ALU.max), inc="dve")
                        W("vector", "dve", cnt["dve"])
                        OP("vector", lambda e, dst=dst, ft=ft, par=par: e.scalar_tensor_tensor(
                            bf(dst + ft * 512, 512), f32(XCS + par * 1024, 512), 1.0,
                            f32(XCS + par * 1024 + 512, 512), ALU.min, ALU.add), inc="dve")
                        evac_act[bidx] = a_exp
                        evac_dve[bidx] = cnt["dve"]
                        elu_dve[par] = cnt["dve"]
                for j in range(4):
                    for half in range(2):
                        bidx = PROJ_BANKS[bi % 4]
                        bi += 1
                        bank = PB[bidx]
                        if bidx in evac_dve:
                            W("tensor", "dve", evac_dve[bidx])
                        if bidx in evac_act:
                            W("tensor", "act", evac_act[bidx])
                        for ki in range(ND):
                            OP("tensor", lambda e, ki=ki, j=j, half=half, bank=bank: e.matmul(
                                bank[:, :], bf(LN2T + ki * 512 + j * 128, 128),
                                bf(WVT + ki * 1024 + half * 512, 512),
                                start=(ki == 0), stop=(ki == ND - 1)))
                        OP("tensor", lambda e: e.matmul(
                            PB[6][:, 0:512], iden[:, :], bf(LN2T, 512),
                            start=True, stop=True), inc="pe")
                        v_pe = cnt["pe"]
                        W("vector", "pe", v_pe)
                        OP("vector", lambda e, j=j, half=half, bank=bank: e.scalar_tensor_tensor(
                            bf(VSB + j * 1024 + half * 512, 512), bank[:, :], 0.0,
                            lstg[:, half * 512:(half + 1) * 512], ALU.bypass, ALU.add),
                           inc="dve")
                        evac_dve[bidx] = cnt["dve"]
                proj_dve = cnt["dve"]
                projend[b] = cnt["pe"]
                if PARANOIA >= 2:
                    BAR()

                ABANKS = (PB[0], PB[1], PB[4], PB[5])
                for cc in range(NCH):
                    cg = b * NCH + cc
                    W("tensor", "dve", proj_dve if cc == 0 else core_dve_end)
                    for ft in range(ND):
                        OP("tensor", lambda e, ft=ft, cc=cc: e.transpose(
                            PT[:, ft * 128:(ft + 1) * 128],
                            bf(KF + ft * 512 + cc * 128, 128), iden[:]),
                           inc="pe" if ft == ND - 1 else None)
                    ktr_pe = cnt["pe"]
                    if PARANOIA >= 2:
                        BAR()

                    atev = {}
                    at_pe = {}
                    for h in range(HEADS):
                        bank = PB[h % 2]
                        hb = (h % 2) * 64
                        if h >= 2:
                            W("tensor", "dve", atev[h - 2])
                        OP("tensor", lambda e, bank=bank, h=h, cc=cc, hb=hb: e.matmul(
                            bank[:, 0:128],
                            bf(KF + (h // 2) * 512 + cc * 128, 128)[hb:hb + 64, :],
                            bf(QF + (h // 2) * 512 + cc * 128, 128)[hb:hb + 64, :],
                            start=True, stop=True), inc="pe")
                        at_pe[h] = cnt["pe"]
                        if h == 1:
                            W("vector", "pe", at_pe[1])
                            OP("vector", lambda e: e.tensor_copy(bf(KTM, 1024), PT[:, :]),
                               inc="dve")
                            ktm_done = cnt["dve"]
                        if h >= 1:
                            W("vector", "pe", at_pe[h])
                            OP("vector", lambda e, hp=h - 1, bp=PB[(h - 1) % 2]: e.scalar_tensor_tensor(
                                bf(ATS + hp * 128, 128), bp[:, 0:128], 0.0, mask[:],
                                ALU.bypass, ALU.mult), inc="dve")
                            atev[h - 1] = cnt["dve"]
                    W("tensor", "dve", ktm_done)
                    for h in range(HEADS):
                        hb = (h % 2) * 64
                        sbank, scol = (PB[2], h * 64) if h < 8 else (PB[3], (h - 8) * 64)
                        OP("tensor", lambda e, sbank=sbank, scol=scol, h=h, cc=cc, cg=cg, hb=hb: e.matmul(
                            sbank[hb:hb + 64, scol:scol + 64],
                            bf(KTM + h * 64, 64), bf(VSB + cc * 1024 + h * 64, 64),
                            start=False,
                            stop=(cg == NBLK * NCH - 1 and h % 8 == 7),
                            tile_position=(0, hb)),
                           inc="pe" if h == HEADS - 1 else None)
                    s_pe = cnt["pe"]
                    W("vector", "pe", s_pe)
                    OP("vector", lambda e, hp=HEADS - 1, bp=PB[(HEADS - 1) % 2]: e.scalar_tensor_tensor(
                        bf(ATS + hp * 128, 128), bp[:, 0:128], 0.0, mask[:],
                        ALU.bypass, ALU.mult), inc="dve")
                    atev[HEADS - 1] = cnt["dve"]
                    if PARANOIA >= 2:
                        BAR()
                    OP("tensor", lambda e: e.matmul(PB[6][:, 0:32], zer[:, :],
                                                    bf(LNT, 32), start=True, stop=True))
                    for h in range(HEADS):
                        hb = (h % 2) * 64
                        OP("tensor", lambda e, h=h, hb=hb: e.matmul(
                            PB[6][hb:hb + 64, 16 + h:17 + h], bf(KTM + h * 64, 64),
                            ones_c[:, 0:1], start=False, stop=False,
                            tile_position=(0, hb)))
                    for h in range(HEADS):
                        hb = (h % 2) * 64
                        OP("tensor", lambda e, h=h, cc=cc, hb=hb: e.matmul(
                            PB[6][:, h:h + 1],
                            bf(QF + (h // 2) * 512 + cc * 128, 128)[hb:hb + 64, :],
                            bf(ZSB, 16)[hb:hb + 64, h:h + 1], start=False, stop=False))
                    W("tensor", "dve", atev[HEADS - 1])
                    for h in range(HEADS):
                        OP("tensor", lambda e, h=h: e.matmul(
                            PB[6][:, h:h + 1], bf(ATS + h * 128, 128), ones_c[:, 0:1],
                            start=False, stop=(h == HEADS - 1)))
                    OP("tensor", lambda e: e.matmul(
                        PB[0][:, 0:512], iden[:, :], bf(QF, 512),
                        start=True, stop=True), inc="pe")
                    passa_pe = cnt["pe"]

                    W("vector", "pe", passa_pe)
                    OP("vector", lambda e: e.tensor_single_scalar(
                        f32(DCL, 16), PB[6][:, 0:16], ATT_EPS, ALU.max), inc="dve")
                    dcl_done = cnt["dve"]
                    W("scalar", "dve", dcl_done)
                    OP("scalar", lambda e: e.activation(f32(DTOT, 16), f32(DCL, 16), AF.Ln),
                       inc="act")
                    W("scalar", "act", cnt["act"])
                    OP("scalar", lambda e: e.activation(f32(REC, 16), f32(DTOT, 16), AF.Exp,
                                                        scale=-1.0), inc="act")
                    rec_done = cnt["act"]
                    OP("vector", lambda e: e.tensor_tensor(
                        f32(ZF, 16), f32(ZF, 16), PB[6][:, 16:32],
                        ALU.add), inc="dve")
                    zacc_done = cnt["dve"]

                    if PARANOIA >= 2:
                        BAR()
                    W("vector", "act", rec_done)
                    nev = {}
                    n_pe = {}
                    for h in range(HEADS):
                        bank = PB[4] if h % 2 == 0 else PB[5]
                        hb = (h % 2) * 64
                        if h >= 2:
                            W("tensor", "dve", nev[h - 2])
                        OP("tensor", lambda e, bank=bank, h=h, cc=cc: e.matmul(
                            bank[:, 0:64], bf(ATS + h * 128, 128),
                            bf(VSB + cc * 1024 + h * 64, 64), start=True, stop=False))
                        OP("tensor", lambda e, bank=bank, h=h, cc=cc, hb=hb: e.matmul(
                            bank[:, 0:64],
                            bf(QF + (h // 2) * 512 + cc * 128, 128)[hb:hb + 64, :],
                            bf(SSB + h * 64, 64)[hb:hb + 64, :], start=False, stop=True), inc="pe")
                        n_pe[h] = cnt["pe"]
                        if h >= 1:
                            W("vector", "pe", n_pe[h])
                            OP("vector", lambda e, hp=h - 1, bp=PB[4] if (h - 1) % 2 == 0 else PB[5], cc=cc: e.tensor_scalar(
                                bf(OTM + cc * 1024 + hp * 64, 64), bp[:, 0:64],
                                f32(REC + hp, 1), None, ALU.mult), inc="dve")
                            nev[h - 1] = cnt["dve"]
                    OP("tensor", lambda e: e.matmul(
                        PB[0][:, 0:512], iden[:, :], bf(QF, 512),
                        start=True, stop=True), inc="pe")
                    passb_pe = cnt["pe"]
                    W("vector", "pe", passb_pe)
                    OP("vector", lambda e, hp=HEADS - 1, cc=cc: e.tensor_scalar(
                        bf(OTM + cc * 1024 + hp * 64, 64), PB[5][:, 0:64],
                        f32(REC + hp, 1), None, ALU.mult), inc="dve")
                    nev[HEADS - 1] = cnt["dve"]

                    W("vector", "pe", passb_pe)
                    OP("vector", lambda e: e.tensor_copy(bf(SSB, 512), PB[2][:, :]),
                       inc="dve")
                    OP("vector", lambda e: e.tensor_copy(bf(SSB + 512, 512), PB[3][:, :]),
                       inc="dve")
                    OP("vector", lambda e: e.tensor_copy(bf(ZSB, 16), f32(ZF, 16)),
                       inc="dve")
                    core_dve_end = cnt["dve"]
                    W("tensor", "dve", max(zacc_done, core_dve_end))
                    if PARANOIA >= 2:
                        BAR()

                oev = {}
                for j in range(4):
                    W("tensor", "dve", core_dve_end if j == 0 else oev[j - 1])
                    for ft in range(ND):
                        OP("tensor", lambda e, j=j, ft=ft: e.transpose(
                            PT[:, ft * 128:(ft + 1) * 128],
                            bf(OTM + j * 1024 + ft * 128, 128), iden[:]))
                    OP("tensor", lambda e: e.matmul(
                        PB[6][:, 0:512], iden[:, :], bf(OTM, 512),
                        start=True, stop=True), inc="pe")
                    otr_pe = cnt["pe"]
                    W("vector", "pe", otr_pe)
                    OP("vector", lambda e, j=j: e.tensor_copy(
                        ABF[:, OFM:OFM + 4096].rearrange("p (f t) -> p f t", f=8)[:, :, j * 128:(j + 1) * 128],
                        PT[:, :].rearrange("p (f t) -> p f t", f=8)), inc="dve")
                    oev[j] = cnt["dve"]
                ofm_done = cnt["dve"]
                wo_stt_ms = {}
                W("tensor", "dve", ofm_done)
                for j in range(4):
                    t = b * 4 + j
                    for half in range(2):
                        bkey = (j * 2 + half) % 2
                        bank = PB[bkey]
                        if bkey in wo_stt_ms:
                            W("tensor", "dve", wo_stt_ms[bkey])
                        for ki in range(ND):
                            OP("tensor", lambda e, ki=ki, j=j, half=half, bank=bank: e.matmul(
                                bank[:, :], bf(OFM + ki * 512 + j * 128, 128),
                                bf(WOT + ki * 1024 + half * 512, 512),
                                start=(ki == 0), stop=(ki == ND - 1)))
                        OP("tensor", lambda e: e.matmul(
                            PB[6][:, 0:512], iden[:, :], bf(OFM, 512),
                            start=True, stop=True), inc="pe")
                        w_pe = cnt["pe"]
                        W("vector", "pe", w_pe)
                        OP("vector", lambda e, t=t, half=half, bank=bank: e.scalar_tensor_tensor(
                            f32(R0 + t * D + half * 512, 512), bank[:, :], 0.0,
                            f32(R0 + t * D + half * 512, 512), ALU.bypass, ALU.add),
                           inc="dve")
                        W("vector", "dve", cnt["dve"])
                        OP("vector", lambda e, t=t, half=half: e.tensor_tensor(
                            f32(R0 + t * D + half * 512, 512),
                            f32(R0 + t * D + half * 512, 512),
                            lstg[:, 1024 + half * 512:1024 + (half + 1) * 512], ALU.add),
                           inc="dve")
                        wo_stt_ms[bkey] = cnt["dve"]
                prev_block_dve = cnt["dve"]
                if PARANOIA >= 2:
                    BAR()
            if DBG_ATTN:
                dbgwq = nc.dram_tensor("dbgwq", [128, 8192], BF16, kind="ExternalOutput").ap()
                dbgwk = nc.dram_tensor("dbgwk", [128, 8192], BF16, kind="ExternalOutput").ap()
                dbgwv = nc.dram_tensor("dbgwv", [128, 8192], BF16, kind="ExternalOutput").ap()
                dbgwo = nc.dram_tensor("dbgwo", [128, 8192], BF16, kind="ExternalOutput").ap()
                dbgln = nc.dram_tensor("dbgln2", [T, D], BF16, kind="ExternalOutput").ap()
                dbgbq = nc.dram_tensor("dbgbq", [128, 16], F32, kind="ExternalOutput").ap()
                W("sync", "dve", cnt["dve"])
                W("sync", "pe", cnt["pe"])
                OP("sync", lambda e: e.dma_start(dbgwq, bf(WQT, 8192)), inc="dls", n=16)
                OP("sync", lambda e: e.dma_start(dbgwk, bf(WKT, 8192)), inc="dls", n=16)
                OP("sync", lambda e: e.dma_start(dbgwv, bf(WVT, 8192)), inc="dls", n=16)
                OP("sync", lambda e: e.dma_start(dbgwo, bf(WOT, 8192)), inc="dls", n=16)
                OP("sync", lambda e: e.dma_start(dbgln, lnd), inc="dls", n=16)
                OP("sync", lambda e: e.dma_start(dbgbq, f32(BQC, 16)), inc="dls", n=16)
                dbgdcl = nc.dram_tensor("dbgdcl", [128, 16], F32, kind="ExternalOutput").ap()
                dbgrec = nc.dram_tensor("dbgrec", [128, 16], F32, kind="ExternalOutput").ap()
                dbgl2 = nc.dram_tensor("dbgl2", [128, 4096], BF16, kind="ExternalOutput").ap()
                dbgofm = nc.dram_tensor("dbgofm", [128, 4096], BF16, kind="ExternalOutput").ap()
                OP("sync", lambda e: e.dma_start(dbgofm, bf(OFM, 4096)), inc="dls", n=16)
                OP("sync", lambda e: e.dma_start(dbgdcl, f32(DCL, 16)), inc="dls", n=16)
                OP("sync", lambda e: e.dma_start(dbgrec, f32(REC, 16)), inc="dls", n=16)
                OP("sync", lambda e: e.dma_start(dbgl2, bf(LN2T, 4096)), inc="dls", n=16)
                dbgs = nc.dram_tensor("dbgs", [128, 1024], BF16, kind="ExternalOutput").ap()
                dbgz = nc.dram_tensor("dbgz", [128, 16], BF16, kind="ExternalOutput").ap()
                dbgq = nc.dram_tensor("dbgq", [128, 4096], BF16, kind="ExternalOutput").ap()
                dbgk = nc.dram_tensor("dbgk", [128, 4096], BF16, kind="ExternalOutput").ap()
                dbgv = nc.dram_tensor("dbgv", [128, 4096], BF16, kind="ExternalOutput").ap()
                dbgat = nc.dram_tensor("dbgat", [128, 2048], BF16, kind="ExternalOutput").ap()
                dbgo = nc.dram_tensor("dbgo", [128, 4096], BF16, kind="ExternalOutput").ap()
                W("sync", "dve", cnt["dve"])
                W("sync", "pe", cnt["pe"])
                for nm, src in (("s", (SSB, 1024)), ("z", (ZSB, 16)), ("q", (QF, 4096)),
                                ("k", (KF, 4096)), ("v", (VSB, 4096)), ("at", (ATS, 2048)),
                                ("o", (OTM, 4096))):
                    dst = {"s": dbgs, "z": dbgz, "q": dbgq, "k": dbgk, "v": dbgv,
                           "at": dbgat, "o": dbgo}[nm]
                    OP("sync", lambda e, d=dst, s=src: e.dma_start(d, bf(*s)), inc="dls", n=16)
            return cnt["dve"], cnt["pe"], cnt["act"]

        if DBG_LN:
            dbgln = nc.dram_tensor("dbgln", [T, D], BF16, kind="ExternalOutput").ap()
        r_dve, r_pe, r_act = 0, 0, bias_done
        if "ffn1" in PHASES:
            ln1 = emit_ln()
            if DBG_LN:
                W("sync", "dls", ln1)
                OP("sync", lambda e: e.dma_start(dbgln, lnd), inc="dls", n=16)
            r_dve, r_pe, r_act = emit_ffn(ln1, first=True, pre_act=bias_done)
        if "attn" in PHASES:
            ln2 = emit_ln(wait_dve=r_dve, wait_pe=r_pe, wait_act=r_act)
            r_dve, r_pe, r_act = emit_attn(ln2, r_pe, r_dve, r_act)
        if "ffn2" in PHASES:
            ln3 = emit_ln(wait_dve=r_dve, wait_pe=r_pe, wait_act=r_act)
            r_dve, r_pe, r_act = emit_ffn(ln3, first=False, pre_pe=r_pe,
                                          pre_dve=r_dve, pre_act=r_act)

        W("sync", "dve", r_dve)
        W("sync", "pe", r_pe)
        OP("sync", lambda e: e.dma_start(
            out.rearrange("(t p) d -> p t d", p=128),
            f32(R0, 16384).rearrange("p (t d) -> p t d", t=NT)), inc="dls", n=16)
        W("sync", "dls", cnt["dls"])

        @block.sync
        def _(e):
            for f in ops["sync"]:
                f(e)

        @block.gpsimd
        def _(e):
            for f in ops["gpsimd"]:
                f(e)

        @block.vector
        def _(e):
            for f in ops["vector"]:
                f(e)

        @block.scalar
        def _(e):
            for f in ops["scalar"]:
                f(e)

        @block.tensor
        def _(e):
            for f in ops["tensor"]:
                f(e)

    return nc


_NC_CACHE = {}
_LAST_RES = None


def kernel(**inputs):
    key = tuple(PHASES)
    if key not in _NC_CACHE:
        _NC_CACHE[key] = build_nc()
    nc = _NC_CACHE[key]

    x = np.ascontiguousarray(np.asarray(inputs["L0_attn_x"], dtype=np.float32))
    base = {
        "wq": np.ascontiguousarray(np.asarray(inputs["L0_attn_Wq"], dtype=np.float32)),
        "wk": np.ascontiguousarray(np.asarray(inputs["L0_attn_Wk"], dtype=np.float32)),
        "wv": np.ascontiguousarray(np.asarray(inputs["L0_attn_Wv"], dtype=np.float32)),
        "wo": np.ascontiguousarray(np.asarray(inputs["L0_attn_Wo"], dtype=np.float32)),
        "bq": np.ascontiguousarray(np.asarray(inputs["L0_attn_bq"], dtype=np.float32)),
        "bk": np.ascontiguousarray(np.asarray(inputs["L0_attn_bk"], dtype=np.float32)),
        "bv": np.ascontiguousarray(np.asarray(inputs["L0_attn_bv"], dtype=np.float32)),
        "bo": np.ascontiguousarray(np.asarray(inputs["L0_attn_bo"], dtype=np.float32)),
        "wg": np.ascontiguousarray(np.asarray(inputs["L0_ffn_w_gate"], dtype=np.int32)),
        "wu": np.ascontiguousarray(np.asarray(inputs["L0_ffn_w_up"], dtype=np.int32)),
        "wd": np.ascontiguousarray(np.asarray(inputs["L0_ffn_w_down"], dtype=np.int32)),
    }
    in_maps = [dict(base, x=np.ascontiguousarray(x[i])) for i in range(8)]
    res = run_bass_kernel_spmd(nc, in_maps, core_ids=list(range(8)))
    global _LAST_RES
    _LAST_RES = res
    return np.stack([np.asarray(res.results[i]["out"], dtype=np.float32) for i in range(8)])
